# version 18
# baseline (speedup 1.0000x reference)
import numpy as np
import concourse.bass as bass
import concourse.mybir as mybir
from concourse.bass_utils import run_bass_kernel_spmd
from contextlib import ExitStack

F32 = mybir.dt.float32
BF16 = mybir.dt.bfloat16
I8 = mybir.dt.int8
F16 = mybir.dt.float16
AF = mybir.ActivationFunctionType
ALU = mybir.AluOpType

NPASS = 2
T = 512
NCH = 32
EPS_LN = 1e-5
EPS_AT = 1e-6
WOF_RING = 9


class Ctr:
    def __init__(self, h):
        self.h = h
        self.n = 0

    def bump(self, k=1):
        self.n += k
        return self.n


def build_program():
    nc = bass.Bass("TRN2", target_bir_lowering=False, debug=False)

    xin = nc.dram_tensor("xin", [NPASS, NCH, 128, T], F32, kind="ExternalInput").ap()
    wfD = nc.dram_tensor("wf", [32, 4, 128, 8, 128], I8, kind="ExternalInput").ap()
    wqD = nc.dram_tensor("wqk", [8, 4, 128, 8, 128], F32, kind="ExternalInput").ap()
    woD = nc.dram_tensor("wok", [4, 32, 128, 8, 128], F32, kind="ExternalInput").ap()
    mD = nc.dram_tensor("mk", [128, 8, 128], F32, kind="ExternalInput").ap()
    zD = nc.dram_tensor("zk", [128, 8], F32, kind="ExternalInput").ap()
    bqD = nc.dram_tensor("bqk", [128, 8], F32, kind="ExternalInput").ap()
    scD = nc.dram_tensor("sck", [128, 32], F16, kind="ExternalInput").ap()
    biD = nc.dram_tensor("bik", [128, 32], F16, kind="ExternalInput").ap()
    outD = nc.dram_tensor("out", [NPASS, NCH, 128, T], F32, kind="ExternalOutput").ap()
    scrD = nc.dram_tensor("wof_scr", [32, 128, 1024], BF16, kind="Internal").ap()

    es = ExitStack()
    sb = lambda nm, sh, dt: es.enter_context(nc.sbuf_tensor(nm, sh, dt))
    ps = lambda nm: es.enter_context(nc.psum_tensor(nm, [128, 512], F32))

    xf = sb("xf", [128, NCH * T], F32)
    xb = sb("xb", [128, NCH * T], BF16)
    wt = [sb(f"wt{i}", [128, 4096], BF16) for i in range(2)]
    wqt = [sb(f"wqt{i}", [128, 4096], BF16) for i in range(2)]
    wo4 = [sb(f"wo4{i}", [128, 4096], BF16) for i in range(2)]
    wof = [sb(f"wof{i}", [128, 1024], BF16) for i in range(WOF_RING)]
    wst = [sb(f"wst{i}", [128, 1024], BF16) for i in range(2)]
    foldt = sb("foldt", [128, 1024], BF16)
    retr = sb("retr", [128, 8 * T], BF16)
    P = [sb(f"P{i}", [128, 512], F32) for i in range(4)]
    B = [sb(f"B{i}", [128, 512], BF16) for i in range(2)]
    sigT = [sb(f"sig{i}", [128, 512], BF16) for i in range(2)]
    rstd_b = sb("rstd_b", [128, 512], F32)
    nmr_b = sb("nmr_b", [128, 512], F32)
    rdb = sb("rdb", [128, 512], F32)
    mrow = sb("mrow", [1, 512], F32)
    vrow = sb("vrow", [1, 512], F32)
    rstdR = sb("rstdR", [1, 512], F32)
    nmrR = sb("nmrR", [1, 512], F32)
    rdenR = sb("rdenR", [1, 512], F32)
    csW = sb("csW", [128, 32], F32)
    csQ = sb("csQ", [128, 8], F32)
    mT = sb("mT", [128, 8 * 128], BF16)
    zT = sb("zT", [128, 8], BF16)
    bqT = sb("bqT", [128, 8], F32)
    scT = sb("scT", [128, 32], F32)
    biT = sb("biT", [128, 32], F32)
    onesC = sb("onesC", [128, 1], BF16)
    onesK = sb("onesK", [1, 128], F32)

    pA, pB = ps("pA"), ps("pB")
    pC1, pC2 = ps("pC1"), ps("pC2")
    pSU, pSQ = ps("pSU"), ps("pSQ")
    pDE, pNU = ps("pDE"), ps("pNU")

    st = Ctr(es.enter_context(nc.semaphore("st")))
    sv = Ctr(es.enter_context(nc.semaphore("sv")))
    ss = Ctr(es.enter_context(nc.semaphore("ss")))
    semPAR = Ctr(es.enter_context(nc.semaphore("semPAR")))
    semBQ = Ctr(es.enter_context(nc.semaphore("semBQ")))
    semX4 = [Ctr(es.enter_context(nc.semaphore(f"semX4{i}"))) for i in range(4)]
    sg = Ctr(es.enter_context(nc.semaphore("sg")))
    semOUT4 = [Ctr(es.enter_context(nc.semaphore(f"semOUT{i}"))) for i in range(4)]
    semWT = [Ctr(es.enter_context(nc.semaphore(f"semWT{i}"))) for i in range(2)]
    semWQ = [[Ctr(es.enter_context(nc.semaphore(f"semWQ{i}_{h}"))) for h in range(2)]
             for i in range(2)]
    semWO4 = [[Ctr(es.enter_context(nc.semaphore(f"semWO4{i}_{h}"))) for h in range(2)]
              for i in range(2)]
    semWOF = [Ctr(es.enter_context(nc.semaphore(f"semWOF{i}"))) for i in range(WOF_RING)]
    semSCW = [Ctr(es.enter_context(nc.semaphore(f"semSCW{i}"))) for i in range(2)]

    ops = {k: [] for k in ("gpsimd", "sync", "tensor", "vector", "scalar")}
    G = ops["gpsimd"].append
    SY = ops["sync"].append
    TE = ops["tensor"].append
    V = ops["vector"].append
    S = ops["scalar"].append

    def wait(em, ctr, val):
        if val and val > 0:
            em(lambda e, h=ctr.h, v=val: e.wait_ge(h, v))

    def gdma(ctr, out, in_, gates=()):
        for (c2, val) in gates:
            wait(G, c2, val)
        c = ctr.bump(16)
        G(lambda e, o=out, i=in_, h=ctr.h: e.dma_start(out=o, in_=i).then_inc(h, 16))
        return c

    def sdma(ctr, out, in_, gates=()):
        for (c2, val) in gates:
            wait(SY, c2, val)
        c = ctr.bump(16)
        SY(lambda e, o=out, i=in_, h=ctr.h: e.dma_start(out=o, in_=i).then_inc(h, 16))
        return c

    def t_inc():
        c = st.bump()
        prev = ops["tensor"].pop()
        TE(lambda e, pf=prev, h=st.h: pf(e).then_inc(h, 1))
        return c

    def v_op(f, inc=True):
        if inc:
            c = sv.bump()
            V(lambda e, ff=f, h=sv.h: ff(e).then_inc(h, 1))
            return c
        V(f)
        return None

    def VD():
        V(lambda e: e.drain())

    def s_pub():
        c = ss.bump()
        S(lambda e, h=ss.h: e.drain().then_inc(h, 1))
        return c

    gate = {}

    gdma(semPAR, mT[:].rearrange("p (h e) -> p h e", h=8), mD)
    gdma(semPAR, zT[:], zD)
    gdma(semPAR, scT[:], scD)
    gdma(semPAR, biT[:], biD)
    sdma(semBQ, bqT[:], bqD)
    v_op(lambda e: e.memset(onesC[:], 1.0))
    v_ones = v_op(lambda e: e.memset(onesK[:], 1.0))
    wait(S, semPAR, 64)
    S(lambda e: e.mul(scT[:], scT[:], 0.5))
    S(lambda e: e.mul(biT[:], biT[:], 0.5))
    s_pub()

    wo4_dma_count = [None] * 32
    fold_done = [None] * 32
    scrw_done = [None] * 32
    fold_next = [0]
    wo_mm_done = [None] * 32
    NFOLD = 0
    fold_og_map = {}

    def emit_wo4_dma(mc):
        buf = mc % 2
        gl = []
        if mc >= 2:
            gl.append((sv, fold_done[mc - 2]))
            gl.append((st, wo_mm_done[mc - 2]))
        cA = gdma(semWO4[buf][0],
                  wo4[buf][:, 0:2048].rearrange("p (g h j) -> p g h j", g=2, h=8),
                  woD[0:2, mc].rearrange("g d h j -> d g h j"), gl)
        cB = gdma(semWO4[buf][1],
                  wo4[buf][:, 2048:4096].rearrange("p (g h j) -> p g h j", g=2, h=8),
                  woD[2:4, mc].rearrange("g d h j -> d g h j"))
        wo4_dma_count[mc] = (cA, cB)

    def emit_fold(mc):
        slot = wst[mc % 2]
        t4 = wo4[mc % 2]
        wait(V, semWO4[mc % 2][0], wo4_dma_count[mc][0])
        wait(V, semWO4[mc % 2][1], wo4_dma_count[mc][1])
        if mc >= 2:
            wait(V, semSCW[mc % 2], scrw_done[mc - 2])
        v_op(lambda e, o=slot, x=t4: e.tensor_add(o[:], x[:, 0:1024], x[:, 1024:2048]), inc=False)
        v_op(lambda e, x=t4: e.tensor_add(foldt[:], x[:, 2048:3072], x[:, 3072:4096]), inc=False)
        VD()
        fold_done[mc] = v_op(lambda e, o=slot: e.tensor_add(o[:], o[:], foldt[:]))
        scrw_done[mc] = sdma(semSCW[mc % 2], scrD[mc], slot[:],
                             gates=((sv, fold_done[mc]),))
        fold_next[0] += 1

    def finalize_stats(pre_v_gates, bc_gates=()):
        for g in pre_v_gates:
            wait(V, *g)
        v_op(lambda e: e.tensor_scalar_mul(mrow[:], pSU[0:1, :], 1.0 / 4096.0), inc=False)
        v_op(lambda e: e.tensor_scalar_mul(vrow[:], pSQ[0:1, :], 1.0 / 4096.0), inc=False)
        VD()
        v_op(lambda e: e.tensor_mul(rdenR[:], mrow[:], mrow[:]), inc=False)
        VD()
        v_op(lambda e: e.tensor_sub(vrow[:], vrow[:], rdenR[:]), inc=False)
        VD()
        v_var = v_op(lambda e: e.tensor_scalar_add(vrow[:], vrow[:], EPS_LN))
        wait(S, sv, v_var)
        S(lambda e: e.activation(rdenR[:], vrow[:], AF.Sqrt))
        s_sqrt = s_pub()
        wait(V, ss, s_sqrt)
        v_op(lambda e: e.reciprocal(rstdR[:], rdenR[:]), inc=False)
        VD()
        v_rows = v_op(lambda e: e.scalar_tensor_tensor(
            nmrR[:], mrow[:], -1.0, rstdR[:], ALU.mult, ALU.mult))

        def emit_bc():
            wait(TE, sv, v_rows)
            for g in bc_gates:
                wait(TE, *g)
            TE(lambda e: e.matmul(pDE[0:128, :], onesK[:], rstdR[:], start=True, stop=True))
            TE(lambda e: e.matmul(pNU[0:128, :], onesK[:], nmrR[:], start=True, stop=True))
            t_bc = t_inc()
            wait(S, st, t_bc)
            S(lambda e: e.copy(rstd_b[:], pDE[0:128, :]))
            S(lambda e: e.copy(nmr_b[:], pNU[0:128, :]))
            return s_pub()
        return emit_bc, v_var

    def ingest_wave(p, stats_v_gate, xb_guard=None):
        casts = []
        t_after = [0] * 32
        for g in range(8):
            xq = (semX4[g // 2], 16 * (p + 1))
            lo, hi = g * 4 * T, (g + 1) * 4 * T
            if xb_guard:
                wait(S, st, xb_guard[g])
            wait(S, *xq)
            S(lambda e, lo=lo, hi=hi: e.copy(xb[:, lo:hi], xf[:, lo:hi]))
            pub = (ss, s_pub())
            casts.append(pub)
            for k in range(4):
                ch = 4 * g + k
                wait(V, *pub)
                if ch >= 2:
                    wait(V, st, t_after[ch - 2])
                v_sq = v_op(lambda e, ch=ch: e.tensor_mul(
                    B[ch % 2][:], xb[:, ch * T:(ch + 1) * T], xb[:, ch * T:(ch + 1) * T]))
                if ch == 0:
                    wait(TE, sv, 2)
                    if stats_v_gate:
                        wait(TE, *stats_v_gate)
                wait(TE, *pub)
                TE(lambda e, ch=ch: e.matmul(pSU[0:1, :], onesC[:], xb[:, ch * T:(ch + 1) * T],
                                             start=(ch == 0), stop=(ch == 31)))
                wait(TE, sv, v_sq)
                TE(lambda e, ch=ch: e.matmul(pSQ[0:1, :], onesC[:], B[ch % 2][:],
                                             start=(ch == 0), stop=(ch == 31)))
                t_after[ch] = t_inc()
        return casts, t_after

    def recast_wave(og31p, v_x, stats_v_gate):
        casts = []
        t_after = [0] * 32
        for g in range(8):
            lo, hi = g * 4 * T, (g + 1) * 4 * T
            wait(S, st, og31p[g])
            wait(S, sv, v_x[min(4 * g + 3, 31)])
            S(lambda e, lo=lo, hi=hi: e.copy(xb[:, lo:hi], xf[:, lo:hi]))
            pub = (ss, s_pub())
            casts.append(pub)
            for k in range(4):
                ch = 4 * g + k
                wait(V, *pub)
                if ch >= 2:
                    wait(V, st, t_after[ch - 2])
                v_sq = v_op(lambda e, ch=ch: e.tensor_mul(
                    B[ch % 2][:], xb[:, ch * T:(ch + 1) * T], xb[:, ch * T:(ch + 1) * T]))
                if ch == 0 and stats_v_gate:
                    wait(TE, *stats_v_gate)
                wait(TE, *pub)
                TE(lambda e, ch=ch: e.matmul(pSU[0:1, :], onesC[:], xb[:, ch * T:(ch + 1) * T],
                                             start=(ch == 0), stop=(ch == 31)))
                wait(TE, sv, v_sq)
                TE(lambda e, ch=ch: e.matmul(pSQ[0:1, :], onesC[:], B[ch % 2][:],
                                             start=(ch == 0), stop=(ch == 31)))
                t_after[ch] = t_inc()

        return casts, t_after

    def ffn(p, sub, emit_bc, xb_gates, stats_v_gate=None):
        s_bc = None
        do_cs = (p == 0 and sub == 0)
        do_stats = (sub == 0)
        t_og = [0] * 32
        v_x = [0] * 32
        s_t1 = [0] * 32
        s_cse = [0] * 32
        og31p = [0] * 8
        c_out = [0] * 32
        for og in range(32):
            buf = og % 2
            bank = pA if buf == 0 else pB
            pcs = pC1 if buf == 0 else pC2
            gl = [gate[("wt", buf)]] if ("wt", buf) in gate else []
            c_wt = gdma(semWT[buf], wt[buf][:].rearrange("p (b i j) -> p b i j", b=4, i=8),
                        wfD[og].rearrange("b p i j -> p b i j"), gl)
            wait(TE, semWT[buf], c_wt)
            if og == 0:
                for gg in xb_gates[:len(xb_gates) // 2] if len(xb_gates) > 1 else xb_gates:
                    wait(TE, *gg)
            if ("bank", buf) in gate:
                wait(TE, *gate[("bank", buf)])
            if do_cs and ("pcs", buf) in gate:
                wait(TE, *gate[("pcs", buf)])
            for i in range(32):
                if og == 0 and i == 16 and len(xb_gates) > 1:
                    for gg in xb_gates[len(xb_gates) // 2:]:
                        wait(TE, *gg)
                TE(lambda e, b=bank, w=wt[buf], i=i: e.matmul(
                    b[:], w[:, i * 128:(i + 1) * 128], xb[:, i * T:(i + 1) * T],
                    start=(i == 0), stop=(i == 31)))
                if do_cs:
                    TE(lambda e, w=wt[buf], i=i, pc=pcs: e.matmul(
                        pc[:, 0:1], w[:, i * 128:(i + 1) * 128], onesC[:],
                        start=(i == 0), stop=(i == 31)))
                if og == 31 and i % 4 == 3:
                    og31p[i // 4] = t_inc()
            t_og[og] = og31p[7] if og == 31 else t_inc()
            gate[("wt", buf)] = (st, t_og[og])
            if og == 0:
                s_bc = emit_bc()
            if do_cs:
                wait(S, st, t_og[og])
                S(lambda e, og=og, pc=pcs: e.copy(csW[:, og:og + 1], pc[:, 0:1]))
                s_cse[og] = s_pub()
                gate[("pcs", buf)] = (ss, s_cse[og])
            wait(V, st, t_og[og])
            if og == 0:
                wait(V, ss, s_bc)
            if ("P", buf) in gate:
                wait(V, *gate[("P", buf)])
            v_t0 = v_op(lambda e, b=bank, buf=buf: e.tensor_mul(P[buf][:], b[:], rstd_b[:]))
            gate[("bank", buf)] = (sv, v_t0)
            if do_cs:
                wait(V, ss, s_cse[og])
            VD()
            v_t0c = v_op(lambda e, buf=buf, og=og: e.scalar_tensor_tensor(
                P[buf][:], nmr_b[:], csW[:, og:og + 1], P[buf][:], ALU.mult, ALU.add))
            if og >= 1:
                k = og - 1
                wait(V, ss, s_t1[k])
                v_x[k] = v_op(lambda e, k=k: e.tensor_add(
                    xf[:, k * T:(k + 1) * T], xf[:, k * T:(k + 1) * T], P[2 + (k % 2)][:]))
                if sub == 1:
                    c_out[k] = sdma(semOUT4[k // 8], outD[p, k], xf[:, k * T:(k + 1) * T],
                                    gates=((sv, v_x[k]),))

            if do_cs and og == 30:
                emit_wo4_dma(0)
                emit_wo4_dma(1)
            wait(S, sv, v_t0c)
            if og >= 2:
                wait(S, sv, v_x[og - 2])
            S(lambda e, buf=buf, og=og: e.activation(
                P[2 + buf][:], P[buf][:], AF.Identity,
                bias=biT[:, og:og + 1], scale=scT[:, og:og + 1]))
            s_t1[og] = s_pub()
            gate[("P", buf)] = (ss, s_t1[og])

        k = 31
        wait(V, ss, s_t1[k])
        v_x[k] = v_op(lambda e, k=k: e.tensor_add(
            xf[:, k * T:(k + 1) * T], xf[:, k * T:(k + 1) * T], P[2 + (k % 2)][:]))
        if sub == 1:
            c_out[k] = sdma(semOUT4[k // 8], outD[p, k], xf[:, k * T:(k + 1) * T],
                            gates=((sv, v_x[k]),))
        return t_og, v_x, c_out, og31p

    def wq_dma(rc):
        buf = rc % 2
        gl = [gate[("wqt", buf)]] if ("wqt", buf) in gate else []
        cA = gdma(semWQ[buf][0],
                  wqt[buf][:, 0:2048].rearrange("p (b i j) -> p b i j", b=2, i=8),
                  wqD[rc][0:2].rearrange("b p i j -> p b i j"), gl)
        cB = gdma(semWQ[buf][1],
                  wqt[buf][:, 2048:4096].rearrange("p (b i j) -> p b i j", b=2, i=8),
                  wqD[rc][2:4].rearrange("b p i j -> p b i j"))
        gate[("wq_dma", rc)] = (semWQ[buf][0], cA)
        gate[("wq_dmaB", rc)] = (semWQ[buf][1], cB)

    def attn(p, emit_bc, xb_gates, statsrows_gate):
        do_cs = (p == 0)
        s_bc = None
        t_q = [0] * 8
        t_den = [0] * 8
        t_num = [0] * 8
        t_bcr = [0] * 8
        v_sig = [0] * 8
        v_eps = [0] * 8
        v_recip = [0] * 8
        v_retr = [0] * 8
        v_u = [0] * 8
        v_lsig_tail = [0] * 8
        s_e = [0] * 8
        s_qcse = [0] * 8

        nf_pre = NFOLD if p == 0 else 32
        if p == 1:
            pre_read0 = sdma(semWOF[0], wof[0][:], scrD[0],
                             gates=((semSCW[0], semSCW[0].n),))
            pre_read1 = sdma(semWOF[1], wof[1][:], scrD[1],
                             gates=((semSCW[1], semSCW[1].n),))
            gate[("wofr", 0)] = (semWOF[0], pre_read0)
            gate[("wofr", 1)] = (semWOF[1], pre_read1)

        def wo_preamble(mc):
            buf = mc % 2
            bank = pA if buf == 0 else pB
            if ("bank", buf) in gate:
                wait(TE, *gate[("bank", buf)])
            if mc < nf_pre:
                wait(TE, *gate[("wofr", mc)])
                slot = wof[mc % WOF_RING]
                for hg in range(7):
                    wait(TE, sv, v_retr[hg])
                    TE(lambda e, b=bank, s=slot, hg=hg: e.matmul(
                        b[:], s[:, hg * 128:(hg + 1) * 128], retr[:, hg * T:(hg + 1) * T],
                        start=(hg == 0), stop=False))
            else:
                wait(TE, semWO4[buf][0], wo4_dma_count[mc][0])
                t4 = wo4[buf]
                for g4 in range(4):
                    if g4 == 2:
                        wait(TE, semWO4[buf][1], wo4_dma_count[mc][1])
                    for hg in range(7):
                        if g4 == 0:
                            wait(TE, sv, v_retr[hg])
                        TE(lambda e, b=bank, s=t4, g4=g4, hg=hg: e.matmul(
                            b[:], s[:, (g4 * 8 + hg) * 128:(g4 * 8 + hg + 1) * 128],
                            retr[:, hg * T:(hg + 1) * T],
                            start=(g4 == 0 and hg == 0), stop=False))

        for j in range(9):
            k1 = j - 1
            if j == 8:
                wo_preamble(0)
                wo_preamble(1)
            if 0 <= k1 <= 7:
                wait(V, ss, s_e[k1])
                if k1 >= 2:
                    wait(V, st, t_num[k1 - 2])
                V(lambda e, k=k1: e.tensor_scalar(
                    P[2][:], P[k % 2][:], bqT[:, k:k + 1], 0.0, ALU.add, ALU.max))
                VD()
                v_sig[k1] = v_op(lambda e, k=k1: e.tensor_add(sigT[k % 2][:], P[3][:], P[2][:]))
                v_lsig_tail[k1] = v_sig[k1]
            if j < 8:
                buf = j % 2
                bank = pA if buf == 0 else pB
                pcs = pC1 if buf == 0 else pC2
                wait(TE, *gate[("wq_dma", j)])
                if j == 0:
                    for gg in xb_gates[:4]:
                        wait(TE, *gg)
                if ("bank", buf) in gate:
                    wait(TE, *gate[("bank", buf)])
                if do_cs and ("pcs", buf) in gate:
                    wait(TE, *gate[("pcs", buf)])
                for i in range(32):
                    if i == 16:
                        wait(TE, *gate[("wq_dmaB", j)])
                        if j == 0:
                            for gg in xb_gates[4:]:
                                wait(TE, *gg)
                    TE(lambda e, b=bank, w=wqt[buf], i=i: e.matmul(
                        b[:], w[:, i * 128:(i + 1) * 128], xb[:, i * T:(i + 1) * T],
                        start=(i == 0), stop=(i == 31)))
                    if do_cs:
                        TE(lambda e, w=wqt[buf], i=i, pc=pcs: e.matmul(
                            pc[:, 0:1], w[:, i * 128:(i + 1) * 128], onesC[:],
                            start=(i == 0), stop=(i == 31)))
                t_q[j] = t_inc()
                gate[("wqt", buf)] = (st, t_q[j])
                if j == 0:
                    s_bc = emit_bc()
                if j + 2 < 8:
                    wq_dma(j + 2)
                if do_cs:
                    wait(S, st, t_q[j])
                    S(lambda e, j=j, pc=pcs: e.copy(csQ[:, j:j + 1], pc[:, 0:1]))
                    s_qcse[j] = s_pub()
                    gate[("pcs", buf)] = (ss, s_qcse[j])
            if 0 <= k1 <= 7:
                dbank = pDE if k1 % 2 == 0 else pSQ
                wait(TE, sv, v_sig[k1])
                if k1 >= 2:
                    wait(TE, sv, v_eps[k1 - 2])
                if k1 == 0:
                    wait(TE, semPAR, 64)
                    wait(TE, ss, s_bc)
                if k1 == 1:
                    wait(TE, *statsrows_gate)
                TE(lambda e, db=dbank, k=k1: e.matmul(
                    db[0:1, :], zT[:, k:k + 1], sigT[k % 2][:], start=True, stop=True))
                t_den[k1] = t_inc()
                if k1 >= 1:
                    wait(V, st, t_bcr[k1 - 1])
                wait(V, st, t_den[k1])
                v_eps[k1] = v_op(lambda e, db=dbank: e.tensor_scalar_add(rdenR[:], db[0:1, :], EPS_AT))
                if k1 >= 1:
                    wait(TE, sv, v_retr[k1 - 1])
                TE(lambda e, k=k1: e.matmul(
                    pNU[:], mT[:, k * 128:(k + 1) * 128], sigT[k % 2][:], start=True, stop=True))
                t_num[k1] = t_inc()
                wait(TE, sv, v_eps[k1])
                if k1 == 0:
                    wait(TE, *statsrows_gate)
                if k1 >= 1:
                    wait(TE, sv, v_recip[k1 - 1])
                TE(lambda e: e.matmul(pSU[0:128, :], onesK[:], rdenR[:], start=True, stop=True))
                t_bcr[k1] = t_inc()
                wait(V, st, t_bcr[k1])
                v_recip[k1] = v_op(lambda e: e.reciprocal(rdb[:], pSU[0:128, :]))
                VD()
                wait(V, st, t_num[k1])
                v_retr[k1] = v_op(lambda e, k=k1: e.tensor_mul(
                    retr[:, k * T:(k + 1) * T], pNU[:], rdb[:]))
            if j < 8:
                buf = j % 2
                bank = pA if buf == 0 else pB
                wait(V, st, t_q[j])
                if j == 0:
                    wait(V, ss, s_bc)
                    wait(V, semBQ, 16)
                if ("P", buf) in gate:
                    wait(V, *gate[("P", buf)])
                v_t0 = v_op(lambda e, b=bank, buf=buf: e.tensor_mul(P[buf][:], b[:], rstd_b[:]))
                gate[("bank", buf)] = (sv, v_t0)
                if do_cs:
                    wait(V, ss, s_qcse[j])
                VD()
                v_op(lambda e, buf=buf, j=j: e.scalar_tensor_tensor(
                    P[buf][:], nmr_b[:], csQ[:, j:j + 1], P[buf][:], ALU.mult, ALU.add), inc=False)
                VD()
                v_u[j] = v_op(lambda e, buf=buf, j=j: e.tensor_scalar(
                    P[2][:], P[buf][:], bqT[:, j:j + 1], 0.0, ALU.add, ALU.min))
            if j < 8:
                wait(S, sv, v_u[j])
                S(lambda e: e.activation(P[3][:], P[2][:], AF.Exp))
                s_e[j] = s_pub()
        gate[("P", 0)] = (sv, v_lsig_tail[6])
        gate[("P", 1)] = (sv, v_lsig_tail[7])
        gate["lastrecip"] = (sv, v_retr[7])

        v_wx = [0] * 32
        s_wc = [0] * 32
        v_wsq = [0] * 32
        t_stats = [0] * 32
        def scr_read(mc):
            gl = [(st, wo_mm_done[mc - WOF_RING])] if mc >= WOF_RING else []
            gl.append((semSCW[mc % 2], semSCW[mc % 2].n))
            c = sdma(semWOF[mc % WOF_RING], wof[mc % WOF_RING][:], scrD[mc], gl)
            gate[("wofr", mc)] = (semWOF[mc % WOF_RING], c)

        nf = NFOLD if p == 0 else 32
        for q in range(2, 4):
            if q < nf:
                scr_read(q)
        for mc in range(32):
            buf = mc % 2
            bank = pA if buf == 0 else pB
            if mc + 4 < nf:
                scr_read(mc + 4)
            folded = mc < nf
            if folded:
                wait(TE, *gate[("wofr", mc)])
            else:
                wait(TE, semWO4[buf][0], wo4_dma_count[mc][0])
            if mc >= 2 and ("bank", buf) in gate:
                wait(TE, *gate[("bank", buf)])
            if folded:
                slot = wof[mc % WOF_RING]
                if mc < 2:
                    wait(TE, sv, v_retr[7])
                    TE(lambda e, b=bank, s=slot: e.matmul(
                        b[:], s[:, 7 * 128:8 * 128], retr[:, 7 * T:8 * T],
                        start=False, stop=True))
                else:
                    for hg in range(8):
                        TE(lambda e, b=bank, s=slot, hg=hg: e.matmul(
                            b[:], s[:, hg * 128:(hg + 1) * 128], retr[:, hg * T:(hg + 1) * T],
                            start=(hg == 0), stop=(hg == 7)))
            else:
                t4 = wo4[buf]
                if mc < 2:
                    wait(TE, sv, v_retr[7])
                    wait(TE, semWO4[buf][1], wo4_dma_count[mc][1])
                    for g4 in range(4):
                        TE(lambda e, b=bank, s=t4, g4=g4: e.matmul(
                            b[:], s[:, (g4 * 8 + 7) * 128:(g4 * 8 + 8) * 128],
                            retr[:, 7 * T:8 * T],
                            start=False, stop=(g4 == 3)))
                else:
                    for g4 in range(4):
                        if g4 == 2:
                            wait(TE, semWO4[buf][1], wo4_dma_count[mc][1])
                        for hg in range(8):
                            TE(lambda e, b=bank, s=t4, g4=g4, hg=hg: e.matmul(
                                b[:], s[:, (g4 * 8 + hg) * 128:(g4 * 8 + hg + 1) * 128],
                                retr[:, hg * T:(hg + 1) * T],
                                start=(g4 == 0 and hg == 0), stop=(g4 == 3 and hg == 7)))
            t_wo = t_inc()
            wo_mm_done[mc] = t_wo
            if p == 0 and not folded:
                emit_fold(mc)
                if mc + 2 < 32:
                    emit_wo4_dma(mc + 2)
            wait(V, st, t_wo)
            v_wx[mc] = v_op(lambda e, mc=mc, b=bank: e.tensor_add(
                xf[:, mc * T:(mc + 1) * T], xf[:, mc * T:(mc + 1) * T], b[:]))
            gate[("bank", buf)] = (sv, v_wx[mc])
            wait(S, sv, v_wx[mc])
            S(lambda e, mc=mc: e.copy(xb[:, mc * T:(mc + 1) * T], xf[:, mc * T:(mc + 1) * T]))
            s_wc[mc] = s_pub()
            for kk in ([mc - 2] if mc >= 2 else []):
                if kk == 0:
                    wait(TE, sv, v_recip[7])
                wait(TE, ss, s_wc[kk])
                TE(lambda e, kk=kk: e.matmul(pSU[0:1, :], onesC[:], xb[:, kk * T:(kk + 1) * T],
                                             start=(kk == 0), stop=(kk == 31)))
                wait(TE, sv, v_wsq[kk])
                TE(lambda e, kk=kk: e.matmul(pSQ[0:1, :], onesC[:], B[kk % 2][:],
                                             start=(kk == 0), stop=(kk == 31)))
                t_stats[kk] = t_inc()
            wait(V, ss, s_wc[mc])
            if mc >= 2:
                wait(V, st, t_stats[mc - 2])
            v_wsq[mc] = v_op(lambda e, mc=mc: e.tensor_mul(
                B[mc % 2][:], xb[:, mc * T:(mc + 1) * T], xb[:, mc * T:(mc + 1) * T]))
            if mc == 31:
                for kk in (30, 31):
                    wait(TE, ss, s_wc[kk])
                    TE(lambda e, kk=kk: e.matmul(pSU[0:1, :], onesC[:], xb[:, kk * T:(kk + 1) * T],
                                                 start=(kk == 0), stop=(kk == 31)))
                    wait(TE, sv, v_wsq[kk])
                    TE(lambda e, kk=kk: e.matmul(pSQ[0:1, :], onesC[:], B[kk % 2][:],
                                                 start=(kk == 0), stop=(kk == 31)))
                    t_stats[kk] = t_inc()
        return v_wx, s_wc, t_stats

    def load_x(p):
        for q in range(4):
            gts = ()
            if p == 1:
                gts = ((semOUT4[q], semOUT4[q].n),)
            sdma(semX4[q],
                 xf[:, q * 8 * T:(q + 1) * 8 * T].rearrange("p (i t) -> p i t", i=8),
                 xin[p][q * 8:(q + 1) * 8].rearrange("i pp t -> pp i t"),
                 gates=gts)

    prev_ffn2 = [None]
    for p in range(NPASS):
        load_x(p)
        statsrows = gate.get("statsrows")
        casts1, t_after = ingest_wave(
            p, stats_v_gate=statsrows, xb_guard=prev_ffn2[0])
        ebc1, v_var1 = finalize_stats(pre_v_gates=[(st, t_after[31])])
        gate["statsrows"] = (sv, v_var1)

        t_og1, v_x1, _, og31p = ffn(p, 0, ebc1, xb_gates=casts1,
                                    stats_v_gate=gate["statsrows"])
        wq_dma(0)
        wq_dma(1)
        casts2, t_after2 = recast_wave(og31p, v_x1, gate["statsrows"])
        ebc2, v_var2 = finalize_stats(pre_v_gates=[(st, t_after2[31])])
        gate["statsrows"] = (sv, v_var2)

        v_wx, s_wc, t_stats = attn(p, ebc2, xb_gates=casts2,
                                   statsrows_gate=gate["statsrows"])

        ebc3, v_var3 = finalize_stats(pre_v_gates=[(st, t_stats[31])],
                                      bc_gates=[(sv, gate["lastrecip"][1])])
        gate["statsrows"] = (sv, v_var3)

        t_og2, v_x2, c_out, og31p2 = ffn(p, 1, ebc3, xb_gates=[(ss, s_wc[31])])
        prev_ffn2[0] = og31p2

    for q in range(4):
        SY(lambda e, v=semOUT4[q].n, h=semOUT4[q].h: e.wait_ge(h, v))

    block = es.enter_context(nc.Block())

    @block.gpsimd
    def _(g):
        for f in ops["gpsimd"]:
            f(g)

    @block.sync
    def _(s_):
        for f in ops["sync"]:
            f(s_)

    @block.tensor
    def _(t):
        for f in ops["tensor"]:
            f(t)

    @block.vector
    def _(v):
        for f in ops["vector"]:
            f(v)

    @block.scalar
    def _(s_):
        for f in ops["scalar"]:
            f(s_)

    es.close()
    return nc


_NC_CACHE = None


def kernel(L0_attn_hidden_states, L0_attn_Wq, L0_attn_bq, L0_attn_Wk, L0_attn_bk,
           L0_attn_Wv, L0_attn_bv, L0_attn_Wo, L0_attn_M, L0_attn_z,
           L0_ffn_w_q, L0_ffn_scale, L0_ffn_bias):
    global _NC_CACHE
    x = np.asarray(L0_attn_hidden_states)
    Wq = np.asarray(L0_attn_Wq)
    bq = np.asarray(L0_attn_bq)
    Wo = np.asarray(L0_attn_Wo)
    M = np.asarray(L0_attn_M)
    z = np.asarray(L0_attn_z)
    wq_i8 = np.asarray(L0_ffn_w_q)
    scale = np.asarray(L0_ffn_scale)
    bias = np.asarray(L0_ffn_bias)

    wf_dev = np.ascontiguousarray(
        wq_i8.reshape(4, 8, 128, 32, 128).transpose(3, 0, 2, 1, 4))
    Wq_sel = Wq.reshape(32, 128, 4096)[::4]
    wq_dev = np.ascontiguousarray(
        Wq_sel.reshape(8, 128, 4, 8, 128).transpose(0, 2, 4, 3, 1))
    wo_dev = np.ascontiguousarray(
        Wo.reshape(32, 128, 8, 4, 128).transpose(3, 0, 4, 2, 1))
    m_dev = np.ascontiguousarray(M.transpose(1, 0, 2))
    z_dev = np.ascontiguousarray(z.T)
    bq_dev = np.ascontiguousarray(bq.reshape(32, 128)[::4].T)
    sc_dev = np.ascontiguousarray(scale.reshape(32, 128).T)
    bi_dev = np.ascontiguousarray(bias.reshape(32, 128).T)

    xt = x.reshape(8192, 4096)
    in_maps = []
    for c in range(8):
        xc = xt[c * 1024:(c + 1) * 1024]
        xc = np.ascontiguousarray(
            xc.T.reshape(NCH, 128, NPASS, T).transpose(2, 0, 1, 3))
        in_maps.append({
            "xin": xc, "wf": wf_dev, "wqk": wq_dev, "wok": wo_dev,
            "mk": m_dev, "zk": z_dev, "bqk": bq_dev,
            "sck": sc_dev, "bik": bi_dev,
        })

    if _NC_CACHE is None:
        _NC_CACHE = build_program()
    res = run_bass_kernel_spmd(_NC_CACHE, in_maps, core_ids=list(range(8)))

    out = np.empty((8192, 4096), np.float32)
    for c in range(8):
        oc = np.asarray(res.results[c]["out"])
        out[c * 1024:(c + 1) * 1024] = oc.transpose(1, 2, 0, 3).reshape(4096, 1024).T
    return out.reshape(2, 4096, 4096)


# version 19 (final)
# speedup vs baseline: 1.0203x; 1.0203x over previous
import numpy as np
import concourse.bass as bass
import concourse.mybir as mybir
from concourse.bass_utils import run_bass_kernel_spmd
from contextlib import ExitStack

F32 = mybir.dt.float32
BF16 = mybir.dt.bfloat16
I8 = mybir.dt.int8
F16 = mybir.dt.float16
AF = mybir.ActivationFunctionType
ALU = mybir.AluOpType

NPASS = 2
T = 512
NCH = 32
EPS_LN = 1e-5
EPS_AT = 1e-6
WOF_RING = 6


class Ctr:
    def __init__(self, h):
        self.h = h
        self.n = 0

    def bump(self, k=1):
        self.n += k
        return self.n


def build_program():
    nc = bass.Bass("TRN2", target_bir_lowering=False, debug=False)

    xin = nc.dram_tensor("xin", [NPASS, NCH, 128, T], F32, kind="ExternalInput").ap()
    wfD = nc.dram_tensor("wf", [32, 4, 128, 8, 128], I8, kind="ExternalInput").ap()
    wqD = nc.dram_tensor("wqk", [8, 4, 128, 8, 128], F32, kind="ExternalInput").ap()
    woD = nc.dram_tensor("wok", [4, 32, 128, 8, 128], F32, kind="ExternalInput").ap()
    mD = nc.dram_tensor("mk", [128, 8, 128], F32, kind="ExternalInput").ap()
    zD = nc.dram_tensor("zk", [128, 8], F32, kind="ExternalInput").ap()
    bqD = nc.dram_tensor("bqk", [128, 8], F32, kind="ExternalInput").ap()
    scD = nc.dram_tensor("sck", [128, 32], F16, kind="ExternalInput").ap()
    biD = nc.dram_tensor("bik", [128, 32], F16, kind="ExternalInput").ap()
    outD = nc.dram_tensor("out", [NPASS, NCH, 128, T], F32, kind="ExternalOutput").ap()
    scrD = nc.dram_tensor("wof_scr", [32, 128, 1024], BF16, kind="Internal").ap()

    es = ExitStack()
    sb = lambda nm, sh, dt: es.enter_context(nc.sbuf_tensor(nm, sh, dt))
    ps = lambda nm: es.enter_context(nc.psum_tensor(nm, [128, 512], F32))

    xf = sb("xf", [128, NCH * T], F32)
    xb = sb("xb", [128, NCH * T], BF16)
    wt = [sb(f"wt{i}", [128, 4096], BF16) for i in range(2)]
    wqt = [sb(f"wqt{i}", [128, 4096], BF16) for i in range(3)]
    wo4 = [sb(f"wo4{i}", [128, 4096], BF16) for i in range(2)]
    wof = [sb(f"wof{i}", [128, 1024], BF16) for i in range(WOF_RING)]
    wst = [sb("wst0", [128, 1024], BF16)]
    foldt = sb("foldt", [128, 1024], BF16)
    retr = sb("retr", [128, 8 * T], BF16)
    P = [sb(f"P{i}", [128, 512], F32) for i in range(4)]
    B = [sb(f"B{i}", [128, 512], BF16) for i in range(2)]
    sigT = [sb(f"sig{i}", [128, 512], BF16) for i in range(2)]
    rstd_b = sb("rstd_b", [128, 512], F32)
    nmr_b = sb("nmr_b", [128, 512], F32)
    rdb = sb("rdb", [128, 512], F32)
    mrow = sb("mrow", [1, 512], F32)
    vrow = sb("vrow", [1, 512], F32)
    rstdR = sb("rstdR", [1, 512], F32)
    nmrR = sb("nmrR", [1, 512], F32)
    rdenR = sb("rdenR", [1, 512], F32)
    csW = sb("csW", [128, 32], F32)
    csQ = sb("csQ", [128, 8], F32)
    mT = sb("mT", [128, 8 * 128], BF16)
    zT = sb("zT", [128, 8], BF16)
    bqT = sb("bqT", [128, 8], F32)
    scT = sb("scT", [128, 32], F32)
    biT = sb("biT", [128, 32], F32)
    onesC = sb("onesC", [128, 1], BF16)
    onesK = sb("onesK", [1, 128], F32)

    pA, pB = ps("pA"), ps("pB")
    pC1, pC2 = ps("pC1"), ps("pC2")
    pSU, pSQ = ps("pSU"), ps("pSQ")
    pDE, pNU = ps("pDE"), ps("pNU")

    st = Ctr(es.enter_context(nc.semaphore("st")))
    sv = Ctr(es.enter_context(nc.semaphore("sv")))
    ss = Ctr(es.enter_context(nc.semaphore("ss")))
    semPAR = Ctr(es.enter_context(nc.semaphore("semPAR")))
    semBQ = Ctr(es.enter_context(nc.semaphore("semBQ")))
    semX4 = [Ctr(es.enter_context(nc.semaphore(f"semX4{i}"))) for i in range(4)]
    sg = Ctr(es.enter_context(nc.semaphore("sg")))
    semOUT4 = [Ctr(es.enter_context(nc.semaphore(f"semOUT{i}"))) for i in range(4)]
    semWT = [Ctr(es.enter_context(nc.semaphore(f"semWT{i}"))) for i in range(2)]
    semWQ = [[Ctr(es.enter_context(nc.semaphore(f"semWQ{i}_{h}"))) for h in range(2)]
             for i in range(3)]
    semWO4 = [[Ctr(es.enter_context(nc.semaphore(f"semWO4{i}_{h}"))) for h in range(2)]
              for i in range(2)]
    semWOF = [Ctr(es.enter_context(nc.semaphore(f"semWOF{i}"))) for i in range(WOF_RING)]
    semSCW = [Ctr(es.enter_context(nc.semaphore(f"semSCW{i}"))) for i in range(2)]

    ops = {k: [] for k in ("gpsimd", "sync", "tensor", "vector", "scalar")}
    G = ops["gpsimd"].append
    SY = ops["sync"].append
    TE = ops["tensor"].append
    V = ops["vector"].append
    S = ops["scalar"].append

    def wait(em, ctr, val):
        if val and val > 0:
            em(lambda e, h=ctr.h, v=val: e.wait_ge(h, v))

    def gdma(ctr, out, in_, gates=()):
        for (c2, val) in gates:
            wait(G, c2, val)
        c = ctr.bump(16)
        G(lambda e, o=out, i=in_, h=ctr.h: e.dma_start(out=o, in_=i).then_inc(h, 16))
        return c

    def sdma(ctr, out, in_, gates=()):
        for (c2, val) in gates:
            wait(SY, c2, val)
        c = ctr.bump(16)
        SY(lambda e, o=out, i=in_, h=ctr.h: e.dma_start(out=o, in_=i).then_inc(h, 16))
        return c

    def t_inc():
        c = st.bump()
        prev = ops["tensor"].pop()
        TE(lambda e, pf=prev, h=st.h: pf(e).then_inc(h, 1))
        return c

    def v_op(f, inc=True):
        if inc:
            c = sv.bump()
            V(lambda e, ff=f, h=sv.h: ff(e).then_inc(h, 1))
            return c
        V(f)
        return None

    def VD():
        V(lambda e: e.drain())

    def s_pub():
        c = ss.bump()
        S(lambda e, h=ss.h: e.drain().then_inc(h, 1))
        return c

    gate = {}

    gdma(semPAR, mT[:].rearrange("p (h e) -> p h e", h=8), mD)
    gdma(semPAR, zT[:], zD)
    gdma(semPAR, scT[:], scD)
    gdma(semPAR, biT[:], biD)
    sdma(semBQ, bqT[:], bqD)
    v_op(lambda e: e.memset(onesC[:], 1.0))
    v_ones = v_op(lambda e: e.memset(onesK[:], 1.0))
    wait(S, semPAR, 64)
    S(lambda e: e.mul(scT[:], scT[:], 0.5))
    S(lambda e: e.mul(biT[:], biT[:], 0.5))
    s_pub()

    wo4_dma_count = [None] * 32
    fold_done = [None] * 32
    scrw_done = [None] * 32
    fold_next = [0]
    wo_mm_done = [None] * 32
    NFOLD = 0
    fold_og_map = {}

    def emit_wo4_dma(mc):
        buf = mc % 2
        gl = []
        if mc >= 2:
            gl.append((sv, fold_done[mc - 2]))
            gl.append((st, wo_mm_done[mc - 2]))
        cA = gdma(semWO4[buf][0],
                  wo4[buf][:, 0:2048].rearrange("p (g h j) -> p g h j", g=2, h=8),
                  woD[0:2, mc].rearrange("g d h j -> d g h j"), gl)
        cB = gdma(semWO4[buf][1],
                  wo4[buf][:, 2048:4096].rearrange("p (g h j) -> p g h j", g=2, h=8),
                  woD[2:4, mc].rearrange("g d h j -> d g h j"))
        wo4_dma_count[mc] = (cA, cB)

    def emit_fold(mc):
        slot = wst[0]
        t4 = wo4[mc % 2]
        wait(V, semWO4[mc % 2][0], wo4_dma_count[mc][0])
        wait(V, semWO4[mc % 2][1], wo4_dma_count[mc][1])
        if mc >= 1:
            wait(V, semSCW[(mc - 1) % 2], scrw_done[mc - 1])
        v_op(lambda e, o=slot, x=t4: e.tensor_add(o[:], x[:, 0:1024], x[:, 1024:2048]), inc=False)
        v_op(lambda e, x=t4: e.tensor_add(foldt[:], x[:, 2048:3072], x[:, 3072:4096]), inc=False)
        VD()
        fold_done[mc] = v_op(lambda e, o=slot: e.tensor_add(o[:], o[:], foldt[:]))
        scrw_done[mc] = sdma(semSCW[mc % 2], scrD[mc], slot[:],
                             gates=((sv, fold_done[mc]),))
        fold_next[0] += 1

    def finalize_stats(pre_v_gates, bc_gates=()):
        for g in pre_v_gates:
            wait(V, *g)
        v_op(lambda e: e.tensor_scalar_mul(mrow[:], pSU[0:1, :], 1.0 / 4096.0), inc=False)
        v_op(lambda e: e.tensor_scalar_mul(vrow[:], pSQ[0:1, :], 1.0 / 4096.0), inc=False)
        VD()
        v_op(lambda e: e.tensor_mul(rdenR[:], mrow[:], mrow[:]), inc=False)
        VD()
        v_op(lambda e: e.tensor_sub(vrow[:], vrow[:], rdenR[:]), inc=False)
        VD()
        v_var = v_op(lambda e: e.tensor_scalar_add(vrow[:], vrow[:], EPS_LN))
        wait(S, sv, v_var)
        S(lambda e: e.activation(rdenR[:], vrow[:], AF.Sqrt))
        s_sqrt = s_pub()
        wait(V, ss, s_sqrt)
        v_op(lambda e: e.reciprocal(rstdR[:], rdenR[:]), inc=False)
        VD()
        v_rows = v_op(lambda e: e.scalar_tensor_tensor(
            nmrR[:], mrow[:], -1.0, rstdR[:], ALU.mult, ALU.mult))

        def emit_bc():
            wait(TE, sv, v_rows)
            for g in bc_gates:
                wait(TE, *g)
            TE(lambda e: e.matmul(pDE[0:128, :], onesK[:], rstdR[:], start=True, stop=True))
            TE(lambda e: e.matmul(pNU[0:128, :], onesK[:], nmrR[:], start=True, stop=True))
            t_bc = t_inc()
            wait(S, st, t_bc)
            S(lambda e: e.copy(rstd_b[:], pDE[0:128, :]))
            S(lambda e: e.copy(nmr_b[:], pNU[0:128, :]))
            return s_pub()
        return emit_bc, v_var

    def ingest_wave(p, stats_v_gate, xb_guard=None):
        casts = []
        t_after = [0] * 32
        for g in range(8):
            xq = (semX4[g // 2], 16 * (p + 1))
            lo, hi = g * 4 * T, (g + 1) * 4 * T
            if xb_guard:
                wait(S, st, xb_guard[g])
            wait(S, *xq)
            S(lambda e, lo=lo, hi=hi: e.copy(xb[:, lo:hi], xf[:, lo:hi]))
            pub = (ss, s_pub())
            casts.append(pub)
            for k in range(4):
                ch = 4 * g + k
                wait(V, *pub)
                if ch >= 2:
                    wait(V, st, t_after[ch - 2])
                v_sq = v_op(lambda e, ch=ch: e.tensor_mul(
                    B[ch % 2][:], xb[:, ch * T:(ch + 1) * T], xb[:, ch * T:(ch + 1) * T]))
                if ch == 0:
                    wait(TE, sv, 2)
                    if stats_v_gate:
                        wait(TE, *stats_v_gate)
                wait(TE, *pub)
                TE(lambda e, ch=ch: e.matmul(pSU[0:1, :], onesC[:], xb[:, ch * T:(ch + 1) * T],
                                             start=(ch == 0), stop=(ch == 31)))
                wait(TE, sv, v_sq)
                TE(lambda e, ch=ch: e.matmul(pSQ[0:1, :], onesC[:], B[ch % 2][:],
                                             start=(ch == 0), stop=(ch == 31)))
                t_after[ch] = t_inc()
        return casts, t_after

    def recast_wave(og31p, v_x, stats_v_gate):
        casts = []
        t_after = [0] * 32
        for g in range(8):
            lo, hi = g * 4 * T, (g + 1) * 4 * T
            wait(S, st, og31p[g])
            wait(S, sv, v_x[min(4 * g + 3, 31)])
            S(lambda e, lo=lo, hi=hi: e.copy(xb[:, lo:hi], xf[:, lo:hi]))
            pub = (ss, s_pub())
            casts.append(pub)
            for k in range(4):
                ch = 4 * g + k
                wait(V, *pub)
                if ch >= 2:
                    wait(V, st, t_after[ch - 2])
                v_sq = v_op(lambda e, ch=ch: e.tensor_mul(
                    B[ch % 2][:], xb[:, ch * T:(ch + 1) * T], xb[:, ch * T:(ch + 1) * T]))
                if ch == 0 and stats_v_gate:
                    wait(TE, *stats_v_gate)
                wait(TE, *pub)
                TE(lambda e, ch=ch: e.matmul(pSU[0:1, :], onesC[:], xb[:, ch * T:(ch + 1) * T],
                                             start=(ch == 0), stop=(ch == 31)))
                wait(TE, sv, v_sq)
                TE(lambda e, ch=ch: e.matmul(pSQ[0:1, :], onesC[:], B[ch % 2][:],
                                             start=(ch == 0), stop=(ch == 31)))
                t_after[ch] = t_inc()

        return casts, t_after

    def ffn(p, sub, emit_bc, xb_gates, stats_v_gate=None):
        s_bc = None
        do_cs = (p == 0 and sub == 0)
        do_stats = (sub == 0)
        t_og = [0] * 32
        v_x = [0] * 32
        s_t1 = [0] * 32
        s_cse = [0] * 32
        og31p = [0] * 8
        c_out = [0] * 32
        for og in range(32):
            buf = og % 2
            bank = pA if buf == 0 else pB
            pcs = pC1 if buf == 0 else pC2
            gl = [gate[("wt", buf)]] if ("wt", buf) in gate else []
            c_wt = gdma(semWT[buf], wt[buf][:].rearrange("p (b i j) -> p b i j", b=4, i=8),
                        wfD[og].rearrange("b p i j -> p b i j"), gl)
            wait(TE, semWT[buf], c_wt)
            if og == 0:
                for gg in xb_gates:
                    wait(TE, *gg)
            if ("bank", buf) in gate:
                wait(TE, *gate[("bank", buf)])
            if do_cs and ("pcs", buf) in gate:
                wait(TE, *gate[("pcs", buf)])
            for i in range(32):
                TE(lambda e, b=bank, w=wt[buf], i=i: e.matmul(
                    b[:], w[:, i * 128:(i + 1) * 128], xb[:, i * T:(i + 1) * T],
                    start=(i == 0), stop=(i == 31)))
                if do_cs:
                    TE(lambda e, w=wt[buf], i=i, pc=pcs: e.matmul(
                        pc[:, 0:1], w[:, i * 128:(i + 1) * 128], onesC[:],
                        start=(i == 0), stop=(i == 31)))
                if og == 31 and i % 4 == 3:
                    og31p[i // 4] = t_inc()
            t_og[og] = og31p[7] if og == 31 else t_inc()
            gate[("wt", buf)] = (st, t_og[og])
            if og == 0:
                s_bc = emit_bc()
            if do_cs:
                wait(S, st, t_og[og])
                S(lambda e, og=og, pc=pcs: e.copy(csW[:, og:og + 1], pc[:, 0:1]))
                s_cse[og] = s_pub()
                gate[("pcs", buf)] = (ss, s_cse[og])
            wait(V, st, t_og[og])
            if og == 0:
                wait(V, ss, s_bc)
            if ("P", buf) in gate:
                wait(V, *gate[("P", buf)])
            v_t0 = v_op(lambda e, b=bank, buf=buf: e.tensor_mul(P[buf][:], b[:], rstd_b[:]))
            gate[("bank", buf)] = (sv, v_t0)
            if do_cs:
                wait(V, ss, s_cse[og])
            VD()
            v_t0c = v_op(lambda e, buf=buf, og=og: e.scalar_tensor_tensor(
                P[buf][:], nmr_b[:], csW[:, og:og + 1], P[buf][:], ALU.mult, ALU.add))
            if og >= 1:
                k = og - 1
                wait(V, ss, s_t1[k])
                v_x[k] = v_op(lambda e, k=k: e.tensor_add(
                    xf[:, k * T:(k + 1) * T], xf[:, k * T:(k + 1) * T], P[2 + (k % 2)][:]))
                if sub == 1:
                    c_out[k] = sdma(semOUT4[k // 8], outD[p, k], xf[:, k * T:(k + 1) * T],
                                    gates=((sv, v_x[k]),))

            if do_cs and og == 30:
                emit_wo4_dma(0)
                emit_wo4_dma(1)
            wait(S, sv, v_t0c)
            if og >= 2:
                wait(S, sv, v_x[og - 2])
            S(lambda e, buf=buf, og=og: e.activation(
                P[2 + buf][:], P[buf][:], AF.Identity,
                bias=biT[:, og:og + 1], scale=scT[:, og:og + 1]))
            s_t1[og] = s_pub()
            gate[("P", buf)] = (ss, s_t1[og])

        k = 31
        wait(V, ss, s_t1[k])
        v_x[k] = v_op(lambda e, k=k: e.tensor_add(
            xf[:, k * T:(k + 1) * T], xf[:, k * T:(k + 1) * T], P[2 + (k % 2)][:]))
        if sub == 1:
            c_out[k] = sdma(semOUT4[k // 8], outD[p, k], xf[:, k * T:(k + 1) * T],
                            gates=((sv, v_x[k]),))
        return t_og, v_x, c_out, og31p

    def wq_dma(rc):
        buf = rc % 3
        gl = [gate[("wqt", buf)]] if ("wqt", buf) in gate else []
        cA = gdma(semWQ[buf][0],
                  wqt[buf][:, 0:2048].rearrange("p (b i j) -> p b i j", b=2, i=8),
                  wqD[rc][0:2].rearrange("b p i j -> p b i j"), gl)
        cB = gdma(semWQ[buf][1],
                  wqt[buf][:, 2048:4096].rearrange("p (b i j) -> p b i j", b=2, i=8),
                  wqD[rc][2:4].rearrange("b p i j -> p b i j"))
        gate[("wq_dma", rc)] = (semWQ[buf][0], cA)
        gate[("wq_dmaB", rc)] = (semWQ[buf][1], cB)

    def attn(p, emit_bc, xb_gates, statsrows_gate):
        do_cs = (p == 0)
        s_bc = None
        t_q = [0] * 8
        t_den = [0] * 8
        t_num = [0] * 8
        t_bcr = [0] * 8
        v_sig = [0] * 8
        v_eps = [0] * 8
        v_recip = [0] * 8
        v_retr = [0] * 8
        v_u = [0] * 8
        v_lsig_tail = [0] * 8
        s_e = [0] * 8
        s_qcse = [0] * 8

        nf_pre = NFOLD if p == 0 else 32
        if p == 1:
            pre_read0 = sdma(semWOF[0], wof[0][:], scrD[0],
                             gates=((semSCW[0], semSCW[0].n),))
            pre_read1 = sdma(semWOF[1], wof[1][:], scrD[1],
                             gates=((semSCW[1], semSCW[1].n),))
            gate[("wofr", 0)] = (semWOF[0], pre_read0)
            gate[("wofr", 1)] = (semWOF[1], pre_read1)

        def wo_preamble(mc):
            buf = mc % 2
            bank = pA if buf == 0 else pB
            if ("bank", buf) in gate:
                wait(TE, *gate[("bank", buf)])
            if mc < nf_pre:
                wait(TE, *gate[("wofr", mc)])
                slot = wof[mc % WOF_RING]
                for hg in range(7):
                    wait(TE, sv, v_retr[hg])
                    TE(lambda e, b=bank, s=slot, hg=hg: e.matmul(
                        b[:], s[:, hg * 128:(hg + 1) * 128], retr[:, hg * T:(hg + 1) * T],
                        start=(hg == 0), stop=False))
            else:
                wait(TE, semWO4[buf][0], wo4_dma_count[mc][0])
                t4 = wo4[buf]
                for g4 in range(4):
                    if g4 == 2:
                        wait(TE, semWO4[buf][1], wo4_dma_count[mc][1])
                    for hg in range(7):
                        if g4 == 0:
                            wait(TE, sv, v_retr[hg])
                        TE(lambda e, b=bank, s=t4, g4=g4, hg=hg: e.matmul(
                            b[:], s[:, (g4 * 8 + hg) * 128:(g4 * 8 + hg + 1) * 128],
                            retr[:, hg * T:(hg + 1) * T],
                            start=(g4 == 0 and hg == 0), stop=False))

        for j in range(9):
            k1 = j - 1
            if j == 8:
                wo_preamble(0)
                wo_preamble(1)
            if 0 <= k1 <= 7:
                wait(V, ss, s_e[k1])
                if k1 >= 2:
                    wait(V, st, t_num[k1 - 2])
                V(lambda e, k=k1: e.tensor_scalar(
                    P[2][:], P[k % 2][:], bqT[:, k:k + 1], 0.0, ALU.add, ALU.max))
                VD()
                v_sig[k1] = v_op(lambda e, k=k1: e.tensor_add(sigT[k % 2][:], P[3][:], P[2][:]))
                v_lsig_tail[k1] = v_sig[k1]
            if j < 8:
                buf = j % 2
                wbuf = j % 3
                bank = pA if buf == 0 else pB
                pcs = pC1 if buf == 0 else pC2
                wait(TE, *gate[("wq_dma", j)])
                if j == 0:
                    for gg in xb_gates:
                        wait(TE, *gg)
                if ("bank", buf) in gate:
                    wait(TE, *gate[("bank", buf)])
                if do_cs and ("pcs", buf) in gate:
                    wait(TE, *gate[("pcs", buf)])
                for i in range(32):
                    if i == 16:
                        wait(TE, *gate[("wq_dmaB", j)])
                    TE(lambda e, b=bank, w=wqt[wbuf], i=i: e.matmul(
                        b[:], w[:, i * 128:(i + 1) * 128], xb[:, i * T:(i + 1) * T],
                        start=(i == 0), stop=(i == 31)))
                    if do_cs:
                        TE(lambda e, w=wqt[wbuf], i=i, pc=pcs: e.matmul(
                            pc[:, 0:1], w[:, i * 128:(i + 1) * 128], onesC[:],
                            start=(i == 0), stop=(i == 31)))
                t_q[j] = t_inc()
                gate[("wqt", wbuf)] = (st, t_q[j])
                if j == 0:
                    s_bc = emit_bc()
                if j + 3 < 8:
                    wq_dma(j + 3)
                if do_cs:
                    wait(S, st, t_q[j])
                    S(lambda e, j=j, pc=pcs: e.copy(csQ[:, j:j + 1], pc[:, 0:1]))
                    s_qcse[j] = s_pub()
                    gate[("pcs", buf)] = (ss, s_qcse[j])
            if 0 <= k1 <= 7:
                dbank = pDE if k1 % 2 == 0 else pSQ
                wait(TE, sv, v_sig[k1])
                if k1 >= 2:
                    wait(TE, sv, v_eps[k1 - 2])
                if k1 == 0:
                    wait(TE, semPAR, 64)
                    wait(TE, ss, s_bc)
                if k1 == 1:
                    wait(TE, *statsrows_gate)
                TE(lambda e, db=dbank, k=k1: e.matmul(
                    db[0:1, :], zT[:, k:k + 1], sigT[k % 2][:], start=True, stop=True))
                t_den[k1] = t_inc()
                if k1 >= 1:
                    wait(V, st, t_bcr[k1 - 1])
                wait(V, st, t_den[k1])
                v_eps[k1] = v_op(lambda e, db=dbank: e.tensor_scalar_add(rdenR[:], db[0:1, :], EPS_AT))
                if k1 >= 1:
                    wait(TE, sv, v_retr[k1 - 1])
                TE(lambda e, k=k1: e.matmul(
                    pNU[:], mT[:, k * 128:(k + 1) * 128], sigT[k % 2][:], start=True, stop=True))
                t_num[k1] = t_inc()
                wait(TE, sv, v_eps[k1])
                if k1 == 0:
                    wait(TE, *statsrows_gate)
                if k1 >= 1:
                    wait(TE, sv, v_recip[k1 - 1])
                TE(lambda e: e.matmul(pSU[0:128, :], onesK[:], rdenR[:], start=True, stop=True))
                t_bcr[k1] = t_inc()
                wait(V, st, t_bcr[k1])
                v_recip[k1] = v_op(lambda e: e.reciprocal(rdb[:], pSU[0:128, :]))
                VD()
                wait(V, st, t_num[k1])
                v_retr[k1] = v_op(lambda e, k=k1: e.tensor_mul(
                    retr[:, k * T:(k + 1) * T], pNU[:], rdb[:]))
            if j < 8:
                buf = j % 2
                bank = pA if buf == 0 else pB
                wait(V, st, t_q[j])
                if j == 0:
                    wait(V, ss, s_bc)
                    wait(V, semBQ, 16)
                if ("P", buf) in gate:
                    wait(V, *gate[("P", buf)])
                v_t0 = v_op(lambda e, b=bank, buf=buf: e.tensor_mul(P[buf][:], b[:], rstd_b[:]))
                gate[("bank", buf)] = (sv, v_t0)
                if do_cs:
                    wait(V, ss, s_qcse[j])
                VD()
                v_op(lambda e, buf=buf, j=j: e.scalar_tensor_tensor(
                    P[buf][:], nmr_b[:], csQ[:, j:j + 1], P[buf][:], ALU.mult, ALU.add), inc=False)
                VD()
                v_u[j] = v_op(lambda e, buf=buf, j=j: e.tensor_scalar(
                    P[2][:], P[buf][:], bqT[:, j:j + 1], 0.0, ALU.add, ALU.min))
            if j < 8:
                wait(S, sv, v_u[j])
                S(lambda e: e.activation(P[3][:], P[2][:], AF.Exp))
                s_e[j] = s_pub()
        gate[("P", 0)] = (sv, v_lsig_tail[6])
        gate[("P", 1)] = (sv, v_lsig_tail[7])
        gate["lastrecip"] = (sv, v_retr[7])

        v_wx = [0] * 32
        s_wc = [0] * 32
        v_wsq = [0] * 32
        t_stats = [0] * 32
        def scr_read(mc):
            gl = [(st, wo_mm_done[mc - WOF_RING])] if mc >= WOF_RING else []
            gl.append((semSCW[mc % 2], semSCW[mc % 2].n))
            c = sdma(semWOF[mc % WOF_RING], wof[mc % WOF_RING][:], scrD[mc], gl)
            gate[("wofr", mc)] = (semWOF[mc % WOF_RING], c)

        nf = NFOLD if p == 0 else 32
        for q in range(2, 4):
            if q < nf:
                scr_read(q)
        for mc in range(32):
            buf = mc % 2
            bank = pA if buf == 0 else pB
            if mc + 4 < nf:
                scr_read(mc + 4)
            folded = mc < nf
            if folded:
                wait(TE, *gate[("wofr", mc)])
            else:
                wait(TE, semWO4[buf][0], wo4_dma_count[mc][0])
            if mc >= 2 and ("bank", buf) in gate:
                wait(TE, *gate[("bank", buf)])
            if folded:
                slot = wof[mc % WOF_RING]
                if mc < 2:
                    wait(TE, sv, v_retr[7])
                    TE(lambda e, b=bank, s=slot: e.matmul(
                        b[:], s[:, 7 * 128:8 * 128], retr[:, 7 * T:8 * T],
                        start=False, stop=True))
                else:
                    for hg in range(8):
                        TE(lambda e, b=bank, s=slot, hg=hg: e.matmul(
                            b[:], s[:, hg * 128:(hg + 1) * 128], retr[:, hg * T:(hg + 1) * T],
                            start=(hg == 0), stop=(hg == 7)))
            else:
                t4 = wo4[buf]
                if mc < 2:
                    wait(TE, sv, v_retr[7])
                    wait(TE, semWO4[buf][1], wo4_dma_count[mc][1])
                    for g4 in range(4):
                        TE(lambda e, b=bank, s=t4, g4=g4: e.matmul(
                            b[:], s[:, (g4 * 8 + 7) * 128:(g4 * 8 + 8) * 128],
                            retr[:, 7 * T:8 * T],
                            start=False, stop=(g4 == 3)))
                else:
                    for g4 in range(4):
                        if g4 == 2:
                            wait(TE, semWO4[buf][1], wo4_dma_count[mc][1])
                        for hg in range(8):
                            TE(lambda e, b=bank, s=t4, g4=g4, hg=hg: e.matmul(
                                b[:], s[:, (g4 * 8 + hg) * 128:(g4 * 8 + hg + 1) * 128],
                                retr[:, hg * T:(hg + 1) * T],
                                start=(g4 == 0 and hg == 0), stop=(g4 == 3 and hg == 7)))
            t_wo = t_inc()
            wo_mm_done[mc] = t_wo
            if p == 0 and not folded:
                emit_fold(mc)
                if mc + 2 < 32:
                    emit_wo4_dma(mc + 2)
            wait(V, st, t_wo)
            v_wx[mc] = v_op(lambda e, mc=mc, b=bank: e.tensor_add(
                xf[:, mc * T:(mc + 1) * T], xf[:, mc * T:(mc + 1) * T], b[:]))
            gate[("bank", buf)] = (sv, v_wx[mc])
            wait(S, sv, v_wx[mc])
            S(lambda e, mc=mc: e.copy(xb[:, mc * T:(mc + 1) * T], xf[:, mc * T:(mc + 1) * T]))
            s_wc[mc] = s_pub()
            for kk in ([mc - 2] if mc >= 2 else []):
                if kk == 0:
                    wait(TE, sv, v_recip[7])
                wait(TE, ss, s_wc[kk])
                TE(lambda e, kk=kk: e.matmul(pSU[0:1, :], onesC[:], xb[:, kk * T:(kk + 1) * T],
                                             start=(kk == 0), stop=(kk == 31)))
                wait(TE, sv, v_wsq[kk])
                TE(lambda e, kk=kk: e.matmul(pSQ[0:1, :], onesC[:], B[kk % 2][:],
                                             start=(kk == 0), stop=(kk == 31)))
                t_stats[kk] = t_inc()
            wait(V, ss, s_wc[mc])
            if mc >= 2:
                wait(V, st, t_stats[mc - 2])
            v_wsq[mc] = v_op(lambda e, mc=mc: e.tensor_mul(
                B[mc % 2][:], xb[:, mc * T:(mc + 1) * T], xb[:, mc * T:(mc + 1) * T]))
            if mc == 31:
                for kk in (30, 31):
                    wait(TE, ss, s_wc[kk])
                    TE(lambda e, kk=kk: e.matmul(pSU[0:1, :], onesC[:], xb[:, kk * T:(kk + 1) * T],
                                                 start=(kk == 0), stop=(kk == 31)))
                    wait(TE, sv, v_wsq[kk])
                    TE(lambda e, kk=kk: e.matmul(pSQ[0:1, :], onesC[:], B[kk % 2][:],
                                                 start=(kk == 0), stop=(kk == 31)))
                    t_stats[kk] = t_inc()
        return v_wx, s_wc, t_stats

    def load_x(p):
        for q in range(4):
            gts = ()
            if p == 1:
                gts = ((semOUT4[q], semOUT4[q].n),)
            sdma(semX4[q],
                 xf[:, q * 8 * T:(q + 1) * 8 * T].rearrange("p (i t) -> p i t", i=8),
                 xin[p][q * 8:(q + 1) * 8].rearrange("i pp t -> pp i t"),
                 gates=gts)

    prev_ffn2 = [None]
    for p in range(NPASS):
        load_x(p)
        statsrows = gate.get("statsrows")
        casts1, t_after = ingest_wave(
            p, stats_v_gate=statsrows, xb_guard=prev_ffn2[0])
        ebc1, v_var1 = finalize_stats(pre_v_gates=[(st, t_after[31])])
        gate["statsrows"] = (sv, v_var1)

        t_og1, v_x1, _, og31p = ffn(p, 0, ebc1, xb_gates=casts1,
                                    stats_v_gate=gate["statsrows"])
        wq_dma(0)
        wq_dma(1)
        wq_dma(2)
        casts2, t_after2 = recast_wave(og31p, v_x1, gate["statsrows"])
        ebc2, v_var2 = finalize_stats(pre_v_gates=[(st, t_after2[31])])
        gate["statsrows"] = (sv, v_var2)

        v_wx, s_wc, t_stats = attn(p, ebc2, xb_gates=casts2,
                                   statsrows_gate=gate["statsrows"])

        ebc3, v_var3 = finalize_stats(pre_v_gates=[(st, t_stats[31])],
                                      bc_gates=[(sv, gate["lastrecip"][1])])
        gate["statsrows"] = (sv, v_var3)

        t_og2, v_x2, c_out, og31p2 = ffn(p, 1, ebc3, xb_gates=[(ss, s_wc[31])])
        prev_ffn2[0] = og31p2

    for q in range(4):
        SY(lambda e, v=semOUT4[q].n, h=semOUT4[q].h: e.wait_ge(h, v))

    block = es.enter_context(nc.Block())

    @block.gpsimd
    def _(g):
        for f in ops["gpsimd"]:
            f(g)

    @block.sync
    def _(s_):
        for f in ops["sync"]:
            f(s_)

    @block.tensor
    def _(t):
        for f in ops["tensor"]:
            f(t)

    @block.vector
    def _(v):
        for f in ops["vector"]:
            f(v)

    @block.scalar
    def _(s_):
        for f in ops["scalar"]:
            f(s_)

    es.close()
    return nc


_NC_CACHE = None


def kernel(L0_attn_hidden_states, L0_attn_Wq, L0_attn_bq, L0_attn_Wk, L0_attn_bk,
           L0_attn_Wv, L0_attn_bv, L0_attn_Wo, L0_attn_M, L0_attn_z,
           L0_ffn_w_q, L0_ffn_scale, L0_ffn_bias):
    global _NC_CACHE
    x = np.asarray(L0_attn_hidden_states)
    Wq = np.asarray(L0_attn_Wq)
    bq = np.asarray(L0_attn_bq)
    Wo = np.asarray(L0_attn_Wo)
    M = np.asarray(L0_attn_M)
    z = np.asarray(L0_attn_z)
    wq_i8 = np.asarray(L0_ffn_w_q)
    scale = np.asarray(L0_ffn_scale)
    bias = np.asarray(L0_ffn_bias)

    wf_dev = np.ascontiguousarray(
        wq_i8.reshape(4, 8, 128, 32, 128).transpose(3, 0, 2, 1, 4))
    Wq_sel = Wq.reshape(32, 128, 4096)[::4]
    wq_dev = np.ascontiguousarray(
        Wq_sel.reshape(8, 128, 4, 8, 128).transpose(0, 2, 4, 3, 1))
    wo_dev = np.ascontiguousarray(
        Wo.reshape(32, 128, 8, 4, 128).transpose(3, 0, 4, 2, 1))
    m_dev = np.ascontiguousarray(M.transpose(1, 0, 2))
    z_dev = np.ascontiguousarray(z.T)
    bq_dev = np.ascontiguousarray(bq.reshape(32, 128)[::4].T)
    sc_dev = np.ascontiguousarray(scale.reshape(32, 128).T)
    bi_dev = np.ascontiguousarray(bias.reshape(32, 128).T)

    xt = x.reshape(8192, 4096)
    in_maps = []
    for c in range(8):
        xc = xt[c * 1024:(c + 1) * 1024]
        xc = np.ascontiguousarray(
            xc.T.reshape(NCH, 128, NPASS, T).transpose(2, 0, 1, 3))
        in_maps.append({
            "xin": xc, "wf": wf_dev, "wqk": wq_dev, "wok": wo_dev,
            "mk": m_dev, "zk": z_dev, "bqk": bq_dev,
            "sck": sc_dev, "bik": bi_dev,
        })

    if _NC_CACHE is None:
        _NC_CACHE = build_program()
    res = run_bass_kernel_spmd(_NC_CACHE, in_maps, core_ids=list(range(8)))

    out = np.empty((8192, 4096), np.float32)
    for c in range(8):
        oc = np.asarray(res.results[c]["out"])
        out[c * 1024:(c + 1) * 1024] = oc.transpose(1, 2, 0, 3).reshape(4096, 1024).T
    return out.reshape(2, 4096, 4096)
